# version 09
# baseline (speedup 1.0000x reference)
import contextlib
import numpy as np
import concourse.bass as bass
import concourse.mybir as mybir
from concourse.bass_utils import run_bass_kernel_spmd

F32 = mybir.dt.float32
BF16 = mybir.dt.bfloat16
F8 = mybir.dt.float8e4
AF = mybir.ActivationFunctionType
ALU = mybir.AluOpType

W = 8
L = 16384
D = 2048
NST = 16
RNK = 64
E = 96
G = 4
WT = L // G
TB = 512
SW = 1024
NSW = WT // SW
DSH = 256
NBLK = 2
NIC = 48
NDB = 16
EPS = 1e-5

ENGINES = ("sync", "vector", "scalar", "gpsimd", "tensor")


class Track:

    def __init__(self, serialize=False):
        self.serialize = serialize
        self.ops = []

    def op(self, engine, fn, r=(), w=(), dma=False, cc=False):
        inc = 1 if cc else (16 if dma else 0)
        self.ops.append((engine, fn, tuple(r), tuple(w), inc))

    def finalize_and_emit(self, nc, stack):
        n = len(self.ops)
        mk_eng = []
        for (e, fn, r, wr, dinc) in self.ops:
            if dinc:
                assert len(wr) == 1, wr
                mk_eng.append("r_" + wr[0])
            else:
                mk_eng.append(e)
        last_write = {}
        readers = {}
        deps = [dict() for _ in range(n)]
        needed = [False] * n

        def add_dep(i, j):
            ei, ej = mk_eng[i], mk_eng[j]
            iss = self.ops[i][0]
            if ej == iss and self.ops[j][4] == 0:
                return
            cur = deps[i].get(ej, -1)
            if j > cur:
                deps[i][ej] = j

        for i, (e, fn, r, wr, dinc) in enumerate(self.ops):
            if self.serialize and i > 0:
                add_dep(i, i - 1)
            for b in r:
                j = last_write.get(b)
                if j is not None:
                    add_dep(i, j)
            for b in wr:
                j = last_write.get(b)
                if j is not None:
                    add_dep(i, j)
                for (re, j) in readers.get(b, {}).items():
                    add_dep(i, j)
            for b in wr:
                last_write[b] = i
                readers[b] = {}
            for b in r:
                readers.setdefault(b, {})[mk_eng[i]] = i
        for i in range(n):
            for (ej, j) in deps[i].items():
                needed[j] = True
        for i, (e, fn, r, wr, dinc) in enumerate(self.ops):
            if dinc:
                needed[i] = True
        mark_no = [0] * n
        cnt = {}
        for i in range(n):
            if needed[i]:
                inc = self.ops[i][4] or 1
                cnt[mk_eng[i]] = cnt.get(mk_eng[i], 0) + inc
                mark_no[i] = cnt[mk_eng[i]]
        sems = {k: stack.enter_context(nc.semaphore("s_" + k)) for k in cnt}
        block = stack.enter_context(nc.Block())

        known0 = {k: 0 for k in cnt}

        def make_section(ename):
            my_ops = [i for i in range(n) if self.ops[i][0] == ename]

            def section(eng):
                known = dict(known0)
                for i in my_ops:
                    for (ej, j) in sorted(self.ops_deps_items(i)):
                        v = mark_no[j]
                        assert v > 0, (i, j, ej)
                        if known.get(ej, 0) < v:
                            eng.wait_ge(sems[ej], v)
                            known[ej] = v
                    ins = self.ops[i][1](eng)
                    if needed[i]:
                        ins.then_inc(sems[mk_eng[i]], self.ops[i][4] or 1)
                if ename == "sync":
                    for (k, v) in sorted(cnt.items()):
                        eng.wait_ge(sems[k], v)
            return section

        self._deps = deps
        for ename in ENGINES:
            if any(o[0] == ename for o in self.ops) or ename == "sync":
                getattr(block, ename)(make_section(ename))

    def ops_deps_items(self, i):
        return self._deps[i].items()


def build_nc():
    nc = bass.Bass("TRN2", target_bir_lowering=False)
    dt = nc.dram_tensor

    xfm_own = dt("xfm_own", [G, NDB, 128, TB], F32, kind="ExternalInput").ap()
    xfmT = dt("xfmT", [NBLK, 128, L], F32, kind="ExternalInput").ap()
    wxTh = dt("wxTh", [128, NDB, E], F32, kind="ExternalInput").ap()
    wdtT = dt("wdtT", [RNK, DSH], F32, kind="ExternalInput").ap()
    bdt = dt("bdt", [128, NBLK], F32, kind="ExternalInput").ap()
    alog = dt("alog", [128, NBLK * NST], F32, kind="ExternalInput").ap()
    dparam = dt("dparam", [128, NBLK], F32, kind="ExternalInput").ap()
    gwh = dt("gwh", [NIC, 128, D], F32, kind="ExternalInput").ap()
    uwh = dt("uwh", [NIC, 128, D], F32, kind="ExternalInput").ap()
    dwh = dt("dwh", [NIC, 128, D], F32, kind="ExternalInput").ap()
    gs_t = dt("gs_t", [128, NIC], F32, kind="ExternalInput").ap()
    us_t = dt("us_t", [128, NIC], F32, kind="ExternalInput").ap()
    ds_rep = dt("ds_rep", [128, D], F32, kind="ExternalInput").ap()
    out = dt("out", [G, NDB, 128, TB], F32, kind="ExternalOutput").ap()

    wq_g = dt("wq_g", [NIC, 128, D], BF16).ap()
    wq_u = dt("wq_u", [NIC, 128, D], BF16).ap()
    wq_d = dt("wq_d", [NIC, 128, D], BF16).ap()
    xdbl_in = [dt(f"xdbl_in{g}", [E, TB], BF16) for g in range(G)]
    xdbl_out = [dt(f"xdbl_out{g}", [W * E, TB], BF16) for g in range(G)]
    st1_in = [dt(f"st1_in{g}", [2, TB], F32) for g in range(G)]
    st1_out = [dt(f"st1_out{g}", [2 * W, TB], F32) for g in range(G)]
    res1_in = [dt(f"res1_in{g}", [DSH, WT], BF16) for g in range(G)]
    res1_out = [dt(f"res1_out{g}", [W * DSH, WT], BF16) for g in range(G)]
    st2_in = [dt(f"st2_in{g}", [2, WT], F32) for g in range(G)]
    st2_out = [dt(f"st2_out{g}", [2, WT], F32) for g in range(G)]
    m2r2 = [dt(f"m2r2{g}", [2, WT], F32) for g in range(G)]
    rg = [list(range(W))]

    stack = contextlib.ExitStack()
    sb = lambda name, shape, dtype=F32: stack.enter_context(nc.sbuf_tensor(name, shape, dtype))
    ps = lambda name, shape: stack.enter_context(nc.psum_tensor(name, shape, F32))

    with stack:
        s_wx = sb("s_wx", [128, NDB * E], BF16)
        s_wdt = sb("s_wdt", [RNK, DSH], BF16)
        s_gs = sb("s_gs", [128, NIC])
        s_us = sb("s_us", [128, NIC])
        s_acol = sb("s_acol", [128, NBLK * NST])
        s_bdt = sb("s_bdt", [128, NBLK])
        s_dp = sb("s_dp", [128, NBLK])
        s_hc = sb("s_hc", [128, 2 * NBLK * NST])
        s_ones = sb("s_ones", [128, 1], BF16)
        _sd = sb("s_draw", [RNK, SW], BF16)
        s_draw = [_sd, _sd]

        f_x = [sb(f"f_x{i}", [128, SW]) for i in range(NBLK)]
        f_dl = [[sb(f"f_dl{p}{i}", [128, SW]) for i in range(NBLK)] for p in range(2)]
        f_t = [sb(f"f_t{i}", [128, SW]) for i in range(2)]
        f_mr = sb("f_mr", [128, SW])
        f_rr = sb("f_rr", [128, SW])
        f_dA = [sb(f"f_dA{i}", [128, SW]) for i in range(3)]
        f_y = [sb(f"f_y{i}", [128, SW]) for i in range(NBLK)]
        b_ln = [[sb(f"b_ln{p}{i}", [128, SW], BF16) for i in range(NBLK)] for p in range(2)]
        b_w = [[sb(f"b_w{p}{i}", [128, SW], BF16) for i in range(NBLK)] for p in range(2)]
        b_dx = [sb(f"b_dx{i}", [128, SW], BF16) for i in range(2)]
        b_h = [sb(f"b_h{i}", [128, SW], BF16) for i in range(2)]
        b_B = [sb(f"b_B{i}", [128, SW], BF16) for i in range(2)]
        b_C = [sb(f"b_C{i}", [128, SW], BF16) for i in range(2)]
        b_r1 = b_dx
        b_sq2 = b_h

        s_fuse = [sb(f"s_fuse{i}", [128, 8 * TB], BF16) for i in range(6)]
        fuse = [s_fuse[i // 8][:, (i % 8) * TB:(i % 8 + 1) * TB] for i in range(NIC)]
        s_ln2 = [sb(f"s_ln2{i}", [128, 8 * TB], BF16) for i in range(2)]
        ln2t = [s_ln2[i // 8][:, (i % 8) * TB:(i % 8 + 1) * TB] for i in range(NDB)]
        s_wg = [sb(f"s_wg{i}", [128, 1024], BF16) for i in range(4)]
        s_wu = [sb(f"s_wu{i}", [128, 1024], BF16) for i in range(4)]
        s_dwq = [sb(f"s_dwq{i}", [128, 1024], BF16) for i in range(2)]
        s_r1f = [sb(f"s_r1f{i}", [128, TB], BF16) for i in range(2)]
        _gs1 = sb("s_gsil0", [128, TB], BF16)
        s_gsil = [_gs1, _gs1]
        f_pt = sb("f_pt", [128, SW])
        qstage = s_ln2[0][:].bitcast(F32)[:, 0:D]
        dstage = s_ln2[1][:].bitcast(F32)[:, 0:D]
        s_q8 = s_fuse[2][:, 0:1024].bitcast(F8)

        s_xo = [sb(f"s_xo{i}", [128, TB]) for i in range(2)]
        s_t32 = [sb(f"s_t32{i}", [128, TB]) for i in range(2)]
        s_outt = s_t32
        s_x16 = s_gsil[0]
        s_sq16 = sb("s_sq16", [128, TB], BF16)
        s_xdbl = s_gsil[0][0:96, :]
        s_row = [sb(f"s_row{i}", [1, TB]) for i in range(2)]
        _mr1 = sb("s_mrep1", [128, 2 * TB])
        s_mrep1 = _mr1[:, 0:TB]
        s_rrep1 = _mr1[:, TB:2 * TB]
        s_m2rep = s_mrep1
        s_r2rep = s_rrep1
        s_s08 = s_xo[0][0:8, :]
        s_s18 = s_xo[1][0:8, :]
        s_m28 = s_t32[0][0:8, :]
        s_r28 = s_t32[1][0:8, :]

        p_A = ps("p_A", [128, 2048])
        p_B = ps("p_B", [128, 2048])

        import os as _os
        t = Track(serialize=bool(_os.environ.get("KSER")))
        T = t.op
        _cbox = {}

        def _coff():
            if "v" not in _cbox:
                _cbox["v"] = nc.partition_id() * TB
            return _cbox["v"]

        T("sync", lambda e: e.dma_start(out=qstage[:, 0:NDB * E], in_=wxTh.rearrange("p k e -> p (k e)")), r=(), w=("s_ln20",), dma=True)
        T("vector", lambda e: e.tensor_copy(s_wx[:], qstage[:, 0:NDB * E]), r=("s_ln20",), w=("s_wx",))
        T("sync", lambda e: e.dma_start(out=f_t[1][0:RNK, 0:DSH], in_=wdtT), r=(), w=("f_t1",), dma=True)
        T("vector", lambda e: e.tensor_copy(s_wdt[:], f_t[1][0:RNK, 0:DSH]), r=("f_t1",), w=("s_wdt",))
        T("sync", lambda e: e.dma_start(out=s_gs[:], in_=gs_t), w=("s_gs",), dma=True)
        T("sync", lambda e: e.dma_start(out=s_us[:], in_=us_t), w=("s_us",), dma=True)
        T("sync", lambda e: e.dma_start(out=dstage, in_=ds_rep), w=("s_ln21",), dma=True)
        T("sync", lambda e: e.dma_start(out=s_acol[:], in_=alog), w=("s_acol",), dma=True)
        T("sync", lambda e: e.dma_start(out=s_bdt[:], in_=bdt), w=("s_bdt",), dma=True)
        T("sync", lambda e: e.dma_start(out=s_dp[:], in_=dparam), w=("s_dp",), dma=True)
        T("scalar", lambda e: e.activation(s_acol[:], s_acol[:], AF.Exp), r=("s_acol",), w=("s_acol",))
        T("vector", lambda e: e.tensor_single_scalar(s_acol[:], s_acol[:], -1.0, ALU.mult), r=("s_acol",), w=("s_acol",))
        T("vector", lambda e: e.memset(s_ones[:], 1.0), w=("s_ones",))
        T("vector", lambda e: e.memset(s_hc[:], 0.0), w=("s_hc",))

        def P1(g):
            for db in range(NDB):
                xo, xok = s_xo[db % 2], f"s_xo{db % 2}"
                T("sync", lambda e, g=g, db=db, xo=xo: e.dma_start(out=xo[:], in_=xfm_own[g, db]),
                  r=(), w=(xok,), dma=True)
                T("vector", lambda e, xo=xo: e.tensor_copy(s_x16[:], xo[:]), r=(xok,), w=("s_gsil0",))
                T("scalar", lambda e, xo=xo: e.activation(s_sq16[:], xo[:], AF.Square), r=(xok,), w=("s_sq16",))
                T("tensor", lambda e, db=db: e.matmul(p_B[0:1, 1024:1024 + TB], s_ones[:], s_x16[:],
                                                      start=(db == 0), stop=(db == NDB - 1)),
                  r=("s_ones", "s_gsil0"), w=("pB2",))
                T("tensor", lambda e, db=db: e.matmul(p_B[0:1, 1536:1536 + TB], s_ones[:], s_sq16[:],
                                                      start=(db == 0), stop=(db == NDB - 1)),
                  r=("s_ones", "s_sq16"), w=("pB3",))
            T("vector", lambda e: e.tensor_single_scalar(s_row[0][:], p_B[0:1, 1024:1024 + TB], 1.0 / D, ALU.mult),
              r=("pB2",), w=("row0",))
            T("vector", lambda e: e.tensor_single_scalar(s_row[1][:], p_B[0:1, 1536:1536 + TB], 1.0 / D, ALU.mult),
              r=("pB3",), w=("row1",))
            T("scalar", lambda e, g=g: e.dma_start(out=st1_in[g][0:1, :], in_=s_row[0][:]),
              r=("row0",), w=(f"st1_in{g}",), dma=True)
            T("scalar", lambda e, g=g: e.activation(s_row[0][:], s_row[0][:], AF.Square), r=("row0", f"st1_in{g}"), w=("row0",))
            T("vector", lambda e: e.tensor_sub(s_row[1][:], s_row[1][:], s_row[0][:]), r=("row1", "row0"), w=("row1",))
            T("vector", lambda e: e.tensor_single_scalar(s_row[1][:], s_row[1][:], EPS, ALU.add), r=("row1",), w=("row1",))
            T("scalar", lambda e: e.activation(s_row[1][:], s_row[1][:], AF.Ln), r=("row1",), w=("row1",))
            T("vector", lambda e: e.tensor_single_scalar(s_row[1][:], s_row[1][:], -0.5, ALU.mult), r=("row1",), w=("row1",))
            T("scalar", lambda e: e.activation(s_row[1][:], s_row[1][:], AF.Exp), r=("row1",), w=("row1",))
            T("scalar", lambda e, g=g: e.dma_start(out=st1_in[g][1:2, :], in_=s_row[1][:]),
              r=("row1",), w=(f"st1_in{g}",), dma=True)
            if g == 0 and _os.environ.get("KDBG"):
                od = dt("dbgs_st1in", [2, TB], F32, kind="ExternalOutput")
                T("sync", lambda e, od=od, g=g: e.dma_start(out=od.ap(), in_=st1_in[g].ap()[:, :]),
                  r=(f"st1_in{g}",), w=("dbg",), dma=True)
                od2 = dt("dbgs_row01", [2, TB], F32, kind="ExternalOutput")
                T("scalar", lambda e, od2=od2: e.dma_start(out=od2.ap()[0:1, :], in_=s_row[0][:]),
                  r=("row0",), w=("dbg",), dma=True)
                T("scalar", lambda e, od2=od2: e.dma_start(out=od2.ap()[1:2, :], in_=s_row[1][:]),
                  r=("row1",), w=("dbg",), dma=True)
            T("sync", lambda e, g=g: e.dma_start(out=s_mrep1, in_=st1_in[g].ap()[0:1, :].broadcast_to([128, TB])),
              r=(f"st1_in{g}",), w=("s_mrep1",), dma=True)
            T("sync", lambda e, g=g: e.dma_start(out=s_rrep1, in_=st1_in[g].ap()[1:2, :].broadcast_to([128, TB])),
              r=(f"st1_in{g}",), w=("s_rrep1",), dma=True)
            for db in range(NDB):
                xo, xok = s_xo[db % 2], f"s_xo{db % 2}"
                t32, tk = s_t32[db % 2], f"s_t32{db % 2}"
                lk = ln1f_keys[db // 2]
                T("sync", lambda e, g=g, db=db, xo=xo: e.dma_start(out=xo[:], in_=xfm_own[g, db]),
                  r=(), w=(xok,), dma=True)
                T("vector", lambda e, xo=xo, t32=t32: e.tensor_sub(t32[:], xo[:], s_mrep1),
                  r=(xok, "s_mrep1"), w=(tk,))
                T("vector", lambda e, t32=t32, db=db: e.tensor_mul(ln1f[db], t32[:], s_rrep1),
                  r=(tk, "s_rrep1"), w=(lk,))
                T("tensor", lambda e, db=db: e.matmul(p_B[0:E, 512:512 + TB],
                                                      s_wx[:, db * E:(db + 1) * E], ln1f[db],
                                                      start=(db == 0), stop=(db == NDB - 1)),
                  r=("s_wx", lk), w=("pB1",))
            T("scalar", lambda e: e.activation(s_xdbl[:], p_B[0:E, 512:512 + TB], AF.Copy),
              r=("pB1",), w=("s_gsil0",))
            T("scalar", lambda e, g=g: e.dma_start(out=xdbl_in[g][:, :], in_=s_xdbl[:]),
              r=("s_gsil0",), w=(f"xdbl_in{g}",), dma=True)
            T("gpsimd", lambda e, g=g: e.collective_compute(
                "AllGather", ALU.bypass, replica_groups=rg,
                ins=[xdbl_in[g].ap().opt()], outs=[xdbl_out[g].ap().opt()]),
              r=(f"xdbl_in{g}",), w=(f"xdbl_out{g}",), cc=True)
            T("gpsimd", lambda e, g=g: e.collective_compute(
                "AllGather", ALU.bypass, replica_groups=rg,
                ins=[st1_in[g].ap().opt()], outs=[st1_out[g].ap().opt()]),
              r=(f"st1_in{g}",), w=(f"st1_out{g}",), cc=True)

        ln1f_pool = [b_B[0], b_B[1], b_C[0], b_C[1],
                     b_ln[0][0], b_ln[0][1], b_ln[1][0], b_ln[1][1]]
        ln1f_keys = ["b_B0", "b_B1", "b_C0", "b_C1",
                     "b_ln00", "b_ln01", "b_ln10", "b_ln11"]
        ln1f = []
        for i in range(NDB):
            ln1f.append(ln1f_pool[i // 2][:, (i % 2) * TB:(i % 2 + 1) * TB])

        def QUANT():
            for (wsrc, dst, dk, scaled) in ((gwh, wq_g, "wq_g", False), (uwh, wq_u, "wq_u", False),
                                            (dwh, wq_d, "wq_d", True)):
                for ic in range(NIC):
                    wqo = (s_fuse[0], s_fuse[1])[ic % 2][:, 0:D]
                    wk = f"s_fuse{ic % 2}"
                    T("scalar", lambda e, wsrc=wsrc, ic=ic: e.dma_start(out=qstage, in_=wsrc[ic]),
                      r=(), w=("s_ln20",), dma=True)
                    T("gpsimd", lambda e: e.tensor_copy(s_q8, qstage), r=("s_ln20",), w=("s_fuse2",))
                    if scaled:
                        T("gpsimd", lambda e, wqo=wqo: e.tensor_tensor(wqo, s_q8, dstage, ALU.mult),
                          r=("s_fuse2", "s_ln21"), w=(wk,))
                    else:
                        T("gpsimd", lambda e, wqo=wqo: e.tensor_copy(wqo, s_q8), r=("s_fuse2",), w=(wk,))
                    T("scalar", lambda e, dst=dst, ic=ic, wqo=wqo: e.dma_start(out=dst[ic], in_=wqo),
                      r=(wk,), w=(dk,), dma=True)

        def SCAN_PREP(g, s):
            p = s % 2
            r0 = 4 * (s // 2)
            r0 = 2 * s
            dr, drk = s_draw[p], "s_draw"
            T("sync", lambda e, g=g, r0=r0, dr=dr: e.dma_start(
                out=dr[:].rearrange("e (r t) -> e r t", r=2),
                in_=xdbl_out[g].ap()[r0 * E:(r0 + 2) * E, :]
                .rearrange("(r e) t -> e r t", r=2)[0:RNK]),
              r=(f"xdbl_out{g}",), w=(drk,), dma=True)
            T("sync", lambda e, g=g, r0=r0: e.dma_start(
                out=f_mr[:].rearrange("p (r t) -> p r t", r=2),
                in_=st1_out[g].ap()[2 * r0:2 * (r0 + 2), :]
                .rearrange("(r two) t -> two r t", two=2)[0:1].broadcast_to([128, 2, TB])),
              r=(f"st1_out{g}",), w=("f_mr",), dma=True)
            T("sync", lambda e, g=g, r0=r0: e.dma_start(
                out=f_rr[:].rearrange("p (r t) -> p r t", r=2),
                in_=st1_out[g].ap()[2 * r0:2 * (r0 + 2), :]
                .rearrange("(r two) t -> two r t", two=2)[1:2].broadcast_to([128, 2, TB])),
              r=(f"st1_out{g}",), w=("f_rr",), dma=True)
            for blk in range(NBLK):
                fxb, fxk = f_x[blk], f"f_x{blk}"
                fdb, fdk = f_dl[p][blk], f"f_dl{p}{blk}"
                lnb, lnk = b_ln[p][blk], f"b_ln{p}{blk}"
                wb, wk = b_w[p][blk], f"b_w{p}{blk}"
                T("sync", lambda e, g=g, s=s, blk=blk, fxb=fxb: e.dma_start(
                    out=fxb[:], in_=xfmT[blk, :, g * WT + s * SW: g * WT + (s + 1) * SW]),
                  r=(), w=(fxk,), dma=True)
                for sl in range(2):
                    T("tensor", lambda e, blk=blk, sl=sl, dr=dr: e.matmul(
                        p_B[:, sl * 512:(sl + 1) * 512],
                        s_wdt[:, blk * 128:(blk + 1) * 128],
                        dr[:, sl * 512:(sl + 1) * 512],
                        start=True, stop=True),
                      r=("s_wdt", drk), w=(f"pB{sl}",))
                T("scalar", lambda e, blk=blk: e.activation(
                    f_t[0][:], p_B[:, 0:SW], AF.Abs, bias=s_bdt[:, blk:blk + 1]),
                  r=("pB0", "pB1", "s_bdt"), w=("f_t0",))
                T("scalar", lambda e: e.activation(f_t[1][:], f_t[0][:], AF.Exp, scale=-1.0),
                  r=("f_t0",), w=("f_t1",))
                T("scalar", lambda e: e.activation(f_t[0][:], f_t[1][:], AF.Ln, bias=1.0),
                  r=("f_t1",), w=("f_t0",))
                T("vector", lambda e, blk=blk: e.tensor_scalar(
                    f_t[1][:], p_B[:, 0:SW], s_bdt[:, blk:blk + 1], 0.0, ALU.add, ALU.max),
                  r=("pB0", "pB1", "s_bdt"), w=("f_t1",))
                T("vector", lambda e, fdb=fdb: e.tensor_add(fdb[:], f_t[0][:], f_t[1][:]),
                  r=("f_t0", "f_t1"), w=(fdk,))
                T("gpsimd", lambda e, fxb=fxb: e.tensor_sub(f_pt[:], fxb[:], f_mr[:]),
                  r=(fxk, "f_mr"), w=("f_pt",))
                T("gpsimd", lambda e, lnb=lnb: e.tensor_mul(lnb[:], f_pt[:], f_rr[:]),
                  r=("f_pt", "f_rr"), w=(lnk,))
                T("vector", lambda e, fdb=fdb, lnb=lnb, wb=wb: e.tensor_mul(wb[:], fdb[:], lnb[:]),
                  r=(fdk, lnk), w=(wk,))

        def SCAN_MAIN(g, s):
            p = s % 2
            r0 = 2 * s
            drk = "s_draw"
            iters = [(n, blk) for n in range(NST) for blk in range(NBLK)]
            LOOK = 2

            def emit_dA(k):
                n_, blk_ = iters[k]
                idx = blk_ * NST + n_
                q3 = k % 3
                fdb, fdk = f_dl[p][blk_], f"f_dl{p}{blk_}"
                dA, dAk = f_dA[q3], f"f_dA{q3}"
                T("scalar", lambda e, fdb=fdb, dA=dA, idx=idx: e.activation(
                    dA[:], fdb[:], AF.Exp, scale=s_acol[:, idx:idx + 1]),
                  r=(fdk, "s_acol"), w=(dAk,))

            for k0 in range(LOOK):
                emit_dA(k0)
            for k, (n, blk) in enumerate(iters):
                bB, bBk = b_B[n % 2], f"b_B{n % 2}"
                bC, bCk = b_C[n % 2], f"b_C{n % 2}"
                if blk == 0:
                    T("sync", lambda e, g=g, r0=r0, n=n, bB=bB: e.dma_start(
                        out=bB[:].rearrange("p (r t) -> p r t", r=2),
                        in_=xdbl_out[g].ap()[r0 * E:(r0 + 2) * E, :]
                        .rearrange("(r e) t -> e r t", r=2)[RNK + n:RNK + n + 1]
                        .broadcast_to([128, 2, TB])),
                      r=(f"xdbl_out{g}",), w=(bBk,), dma=True)
                    T("sync", lambda e, g=g, r0=r0, n=n, bC=bC: e.dma_start(
                        out=bC[:].rearrange("p (r t) -> p r t", r=2),
                        in_=xdbl_out[g].ap()[r0 * E:(r0 + 2) * E, :]
                        .rearrange("(r e) t -> e r t", r=2)[RNK + NST + n:RNK + NST + n + 1]
                        .broadcast_to([128, 2, TB])),
                      r=(f"xdbl_out{g}",), w=(bCk,), dma=True)
                idx = blk * NST + n
                q3 = k % 3
                q = blk % 2
                yb, yk = f_y[blk], f"f_y{blk}"
                dA, dAk = f_dA[q3], f"f_dA{q3}"
                dx, dxk = b_dx[q], f"b_dx{q}"
                h, hk = b_h[q], f"b_h{q}"
                wb, wk = b_w[p][blk], f"b_w{p}{blk}"
                T("vector", lambda e, wb=wb, bB=bB, dx=dx: e.tensor_mul(dx[:], wb[:], bB[:]),
                  r=(wk, bBk), w=(dxk,))
                T("vector", lambda e, dA=dA, dx=dx, h=h, idx=idx: e.tensor_tensor_scan(
                    h[:], dA[:], dx[:], s_hc[:, 2 * idx + 1:2 * idx + 2], ALU.mult, ALU.add),
                  r=(dAk, dxk, "s_hc"), w=(hk,))
                if k + LOOK < len(iters):
                    emit_dA(k + LOOK)
                T("scalar", lambda e, h=h, idx=idx: e.copy(
                    s_hc[:, 2 * idx:2 * idx + 2], h[:, SW - 2:SW]),
                  r=(hk,), w=("s_hc",))
                T("vector", lambda e, h=h, bC=bC, dx=dx: e.tensor_mul(dx[:], h[:], bC[:]),
                  r=(hk, bCk), w=(dxk,))
                if n == 0:
                    T("vector", lambda e, yb=yb, dx=dx: e.tensor_copy(yb[:], dx[:]),
                      r=(dxk,), w=(yk,))
                else:
                    T("gpsimd", lambda e, yb=yb, dx=dx: e.tensor_add(yb[:], yb[:], dx[:]),
                      r=(dxk, yk), w=(yk,))
                    if g == 0 and s == int(_os.environ.get("KDSW", "0")) and n <= 1 and blk == 0 and _os.environ.get("KDBG"):
                        for nm, tile, rk in ((f"dA_n{n}", dA, dAk), (f"h_n{n}", h, hk),
                                             (f"hC_n{n}", dx, dxk), (f"B_n{n}", bB, bBk)):
                            od = dt("dbgn_" + nm, list(tile.shape), tile.dtype, kind="ExternalOutput")
                            T("sync", lambda e, od=od, tile=tile: e.dma_start(out=od.ap(), in_=tile[:]),
                              r=(rk,), w=("dbg",), dma=True)
            for blk in range(NBLK):
                fxb, fxk = f_x[blk], f"f_x{blk}"
                lnb, lnk = b_ln[p][blk], f"b_ln{p}{blk}"
                yb, yk = f_y[blk], f"f_y{blk}"
                r1, r1k = b_r1[blk], f"b_dx{blk}"
                T("vector", lambda e, lnb=lnb, yb=yb, blk=blk: e.scalar_tensor_tensor(
                    yb[:], lnb[:], s_dp[:, blk:blk + 1], yb[:], ALU.mult, ALU.add),
                  r=(lnk, "s_dp", yk), w=(yk,))
                T("vector", lambda e, fxb=fxb, yb=yb, r1=r1: e.tensor_add(r1[:], fxb[:], yb[:]),
                  r=(fxk, yk), w=(r1k,))
                T("scalar", lambda e, r1=r1, blk=blk: e.activation(b_sq2[blk][:], r1[:], AF.Square),
                  r=(r1k,), w=(f"b_h{blk}",))
                T("scalar", lambda e, g=g, s=s, blk=blk, r1=r1: e.dma_start(
                    out=res1_in[g].ap()[blk * 128:(blk + 1) * 128, s * SW:(s + 1) * SW], in_=r1[:]),
                  r=(r1k,), w=(f"res1_in{g}",), dma=True)
            if g == 0 and s == 0 and _os.environ.get("KDBG"):
                for nm, tile, rk in (("sq0", b_sq2[0], "b_h0"), ("sq1", b_sq2[1], "b_h1"),
                                     ("r1d0", b_r1[0], "b_dx0"), ("r1d1", b_r1[1], "b_dx1")):
                    od = dt("dbgq_" + nm, list(tile.shape), tile.dtype, kind="ExternalOutput")
                    T("sync", lambda e, od=od, tile=tile: e.dma_start(out=od.ap(), in_=tile[:]),
                      r=(rk,), w=("dbg",), dma=True)
            for sl in range(2):
                for blk in range(NBLK):
                    r1, r1k = b_r1[blk], f"b_dx{blk}"
                    T("tensor", lambda e, r1=r1, sl=sl, blk=blk: e.matmul(
                        p_B[0:1, 1024:1536],
                        s_ones[:], r1[:, sl * 512:(sl + 1) * 512],
                        start=(blk == 0), stop=(blk == NBLK - 1)),
                      r=("s_ones", r1k), w=("pB2",))
                    sqt, sqk = b_sq2[blk], f"b_h{blk}"
                    T("tensor", lambda e, sqt=sqt, sl=sl, blk=blk: e.matmul(
                        p_B[0:1, 1536:2048],
                        s_ones[:], sqt[:, sl * 512:(sl + 1) * 512],
                        start=(blk == 0), stop=(blk == NBLK - 1)),
                      r=("s_ones", sqk), w=("pB3",))
                T("vector", lambda e, sl=sl: e.tensor_copy(s_row[0][:, 0:512], p_B[0:1, 1024:1536]),
                  r=("pB2",), w=("row0",))
                T("vector", lambda e, sl=sl: e.tensor_copy(s_row[1][:, 0:512], p_B[0:1, 1536:2048]),
                  r=("pB3",), w=("row1",))
                T("scalar", lambda e, g=g, s=s, sl=sl: e.dma_start(
                    out=st2_in[g].ap()[0:1, s * SW + sl * 512: s * SW + (sl + 1) * 512],
                    in_=s_row[0][:, 0:512]),
                  r=("row0",), w=(f"st2_in{g}",), dma=True)
                T("scalar", lambda e, g=g, s=s, sl=sl: e.dma_start(
                    out=st2_in[g].ap()[1:2, s * SW + sl * 512: s * SW + (sl + 1) * 512],
                    in_=s_row[1][:, 0:512]),
                  r=("row1",), w=(f"st2_in{g}",), dma=True)

        def SCAN_COLL(g):
            if g == 0 and _os.environ.get("KDBG"):
                od = dt("dbgs_st2in", [2, WT], F32, kind="ExternalOutput")
                T("sync", lambda e, od=od, g=g: e.dma_start(out=od.ap(), in_=st2_in[g].ap()[:, :]),
                  r=(f"st2_in{g}",), w=("dbg",), dma=True)
                od3 = dt("dbgs_res1in", [DSH, WT], BF16, kind="ExternalOutput")
                T("sync", lambda e, od3=od3, g=g: e.dma_start(out=od3.ap(), in_=res1_in[g].ap()[:, :]),
                  r=(f"res1_in{g}",), w=("dbg",), dma=True)
            T("gpsimd", lambda e, g=g: e.collective_compute(
                "AllReduce", ALU.add, replica_groups=rg,
                ins=[st2_in[g].ap().opt()], outs=[st2_out[g].ap().opt()]),
              r=(f"st2_in{g}",), w=(f"st2_out{g}",), cc=True)
            T("gpsimd", lambda e, g=g: e.collective_compute(
                "AllGather", ALU.bypass, replica_groups=rg,
                ins=[res1_in[g].ap().opt()], outs=[res1_out[g].ap().opt()]),
              r=(f"res1_in{g}",), w=(f"res1_out{g}",), cc=True)
            T("sync", lambda e, g=g: e.dma_start(
                out=s_s08[:], in_=st2_out[g].ap()[0:1, :].rearrange("one (c t) -> (one c) t", c=8)),
              r=(f"st2_out{g}",), w=("s_xo0",), dma=True)
            T("sync", lambda e, g=g: e.dma_start(
                out=s_s18[:], in_=st2_out[g].ap()[1:2, :].rearrange("one (c t) -> (one c) t", c=8)),
              r=(f"st2_out{g}",), w=("s_xo1",), dma=True)
            T("vector", lambda e: e.tensor_single_scalar(s_m28[:], s_s08[:], 1.0 / D, ALU.mult),
              r=("s_xo0",), w=("s_t320",))
            T("scalar", lambda e: e.activation(s_r28[:], s_m28[:], AF.Square), r=("s_t320",), w=("s_t321",))
            T("vector", lambda e: e.scalar_tensor_tensor(s_r28[:], s_s18[:], 1.0 / D, s_r28[:], ALU.mult, ALU.subtract),
              r=("s_xo1", "s_t321"), w=("s_t321",))
            T("vector", lambda e: e.tensor_single_scalar(s_r28[:], s_r28[:], EPS, ALU.add), r=("s_t321",), w=("s_t321",))
            T("scalar", lambda e: e.activation(s_r28[:], s_r28[:], AF.Ln), r=("s_t321",), w=("s_t321",))
            T("vector", lambda e: e.tensor_single_scalar(s_r28[:], s_r28[:], -0.5, ALU.mult), r=("s_t321",), w=("s_t321",))
            T("scalar", lambda e: e.activation(s_r28[:], s_r28[:], AF.Exp), r=("s_t321",), w=("s_t321",))
            T("scalar", lambda e, g=g: e.dma_start(
                out=m2r2[g].ap()[0:1, :].rearrange("one (c t) -> (one c) t", c=8), in_=s_m28[:]),
              r=("s_t320",), w=(f"m2r2{g}",), dma=True)
            T("scalar", lambda e, g=g: e.dma_start(
                out=m2r2[g].ap()[1:2, :].rearrange("one (c t) -> (one c) t", c=8), in_=s_r28[:]),
              r=("s_t321",), w=(f"m2r2{g}",), dma=True)

        def LN2_PRO(g):
            T("sync", lambda e, g=g: e.dma_start(
                out=s_m2rep[:], in_=m2r2[g].ap()[0:1, bass.ds(_coff(), TB)].broadcast_to([128, TB])),
              r=(f"m2r2{g}",), w=("s_mrep1",), dma=True)
            T("sync", lambda e, g=g: e.dma_start(
                out=s_r2rep[:], in_=m2r2[g].ap()[1:2, bass.ds(_coff(), TB)].broadcast_to([128, TB])),
              r=(f"m2r2{g}",), w=("s_rrep1",), dma=True)
            for db in range(NDB):
                rf, rfk = s_r1f[db % 2], f"s_r1f{db % 2}"
                lnk = f"s_ln2{db // 8}"
                T("sync", lambda e, g=g, db=db, rf=rf: e.dma_start(
                    out=rf[:], in_=res1_out[g].ap()[db * 128:(db + 1) * 128, bass.ds(_coff(), TB)]),
                  r=(f"res1_out{g}",), w=(rfk,), dma=True)
                T("vector", lambda e, rf=rf, db=db: e.tensor_sub(s_outt[0][:], rf[:], s_m2rep),
                  r=(rfk, "s_mrep1"), w=("s_t320",))
                T("vector", lambda e, db=db: e.tensor_mul(ln2t[db], s_outt[0][:], s_r2rep),
                  r=("s_t320", "s_rrep1"), w=(lnk,))

        def GATEUP(g, part):
            for ic in range(12 * part, 12 * part + 12):
                wgl, wgh_, wgk = s_wg[2 * (ic % 2)], s_wg[2 * (ic % 2) + 1], f"s_wg{ic % 2}"
                wul, wuh_, wuk = s_wu[2 * (ic % 2)], s_wu[2 * (ic % 2) + 1], f"s_wu{ic % 2}"
                gsil, gsk = s_gsil[0], "s_gsil0"
                pg = f"pA{2 * (ic % 2)}"
                pu = f"pA{2 * (ic % 2) + 1}"
                pgo = 1024 * (ic % 2)
                T("scalar", lambda e, ic=ic, wgl=wgl: e.dma_start(out=wgl[:], in_=wq_g[ic][:, 0:1024]),
                  r=("wq_g",), w=(wgk,), dma=True)
                T("scalar", lambda e, ic=ic, wgh_=wgh_: e.dma_start(out=wgh_[:], in_=wq_g[ic][:, 1024:2048]),
                  r=("wq_g",), w=(wgk,), dma=True)
                T("scalar", lambda e, ic=ic, wul=wul: e.dma_start(out=wul[:], in_=wq_u[ic][:, 0:1024]),
                  r=("wq_u",), w=(wuk,), dma=True)
                T("scalar", lambda e, ic=ic, wuh_=wuh_: e.dma_start(out=wuh_[:], in_=wq_u[ic][:, 1024:2048]),
                  r=("wq_u",), w=(wuk,), dma=True)
                for db in range(NDB):
                    wt = (wgl, wgh_)[db // 8]
                    T("tensor", lambda e, wt=wt, db=db, pgo=pgo: e.matmul(
                        p_A[:, pgo:pgo + TB], wt[:, (db % 8) * 128:(db % 8) * 128 + 128], ln2t[db],
                        start=(db == 0), stop=(db == NDB - 1)),
                      r=(wgk, f"s_ln2{db // 8}"), w=(pg,))
                for db in range(NDB):
                    wt = (wul, wuh_)[db // 8]
                    T("tensor", lambda e, wt=wt, db=db, pgo=pgo: e.matmul(
                        p_A[:, pgo + TB:pgo + 2 * TB], wt[:, (db % 8) * 128:(db % 8) * 128 + 128], ln2t[db],
                        start=(db == 0), stop=(db == NDB - 1)),
                      r=(wuk, f"s_ln2{db // 8}"), w=(pu,))
                T("scalar", lambda e, ic=ic, gsil=gsil, pgo=pgo: e.activation(
                    gsil[:], p_A[:, pgo:pgo + TB], AF.Silu, scale=s_gs[:, ic:ic + 1]),
                  r=(pg, "s_gs"), w=(gsk,))
                T("vector", lambda e, ic=ic, gsil=gsil, pgo=pgo: e.scalar_tensor_tensor(
                    fuse[ic], p_A[:, pgo + TB:pgo + 2 * TB], s_us[:, ic:ic + 1], gsil[:], ALU.mult, ALU.mult),
                  r=(pu, "s_us", gsk), w=(f"s_fuse{ic // 8}",))

        def DOWN(g):
            for half in range(2):
                for ic in range(NIC):
                    dwt, dwk = s_dwq[ic % 2], f"s_dwq{ic % 2}"
                    T("scalar", lambda e, ic=ic, half=half, dwt=dwt: e.dma_start(
                        out=dwt[:], in_=wq_d[ic][:, half * 1024:(half + 1) * 1024]),
                      r=("wq_d",), w=(dwk,), dma=True)
                    for j in range(8):
                        pt = (p_A, p_B)[j // 4]
                        pk = ("pA", "pB")[j // 4] + str(j % 4)
                        T("tensor", lambda e, dwt=dwt, j=j, ic=ic, pt=pt: e.matmul(
                            pt[:, (j % 4) * 512:(j % 4) * 512 + 512],
                            dwt[:, j * 128:(j + 1) * 128], fuse[ic],
                            start=(ic == 0), stop=(ic == NIC - 1)),
                          r=(dwk, f"s_fuse{ic // 8}"), w=(pk,))
                for j in range(8):
                    db = half * 8 + j
                    pt = (p_A, p_B)[j // 4]
                    pk = ("pA", "pB")[j // 4] + str(j % 4)
                    ot, otk = s_outt[j % 2], f"s_t32{j % 2}"
                    rf, rfk = s_r1f[db % 2], f"s_r1f{db % 2}"
                    T("sync", lambda e, g=g, db=db, rf=rf: e.dma_start(
                        out=rf[:], in_=res1_out[g].ap()[db * 128:(db + 1) * 128, bass.ds(_coff(), TB)]),
                      r=(f"res1_out{g}",), w=(rfk,), dma=True)
                    T("vector", lambda e, pt=pt, j=j, ot=ot, rf=rf: e.tensor_add(
                        ot[:], pt[:, (j % 4) * 512:(j % 4) * 512 + 512], rf[:]),
                      r=(pk, rfk), w=(otk,))
                    T("sync", lambda e, g=g, db=db, ot=ot: e.dma_start(out=out[g, db], in_=ot[:]),
                      r=(otk,), w=("out",), dma=True)

        if _os.environ.get("KDBG"):
            dbg_specs = [
                ("dbg_xdbl0", xdbl_out[0], [W * E, TB], BF16, "xdbl_out0"),
                ("dbg_st10", st1_out[0], [2 * W, TB], F32, "st1_out0"),
                ("dbg_res10", res1_out[0], [W * DSH, WT], BF16, "res1_out0"),
                ("dbg_st20", st2_out[0], [2, WT], F32, "st2_out0"),
                ("dbg_m2r20", m2r2[0], [2, WT], F32, "m2r20"),
                ("dbg_res11", res1_out[1], [W * DSH, WT], BF16, "res1_out1"),
                ("dbg_wq0", wq_g[0], [128, D], BF16, "wq_g"),
                ("dbg_wd0", wq_d[0], [128, D], BF16, "wq_d"),
            ]
            dbg_emit = []
            for (nm, srct, shp, dtp, rk) in dbg_specs:
                ot = dt(nm, shp, dtp, kind="ExternalOutput")
                dbg_emit.append((ot, srct, rk))

        P1(0)
        QUANT()
        P1(1)
        for s in range(NSW):
            SCAN_PREP(0, s)
            if s == int(_os.environ.get("KDSW", "0")) and _os.environ.get("KDBG"):
                dbg_sc = {}
                pp_ = s % 2
                for nm, tile, rk in (("dl0", f_dl[pp_][0], f"f_dl{pp_}0"), ("dl1", f_dl[pp_][1], f"f_dl{pp_}1"),
                                     ("ln0", b_ln[pp_][0], f"b_ln{pp_}0"), ("ln1", b_ln[pp_][1], f"b_ln{pp_}1"),
                                     ("w0", b_w[pp_][0], f"b_w{pp_}0"), ("w1", b_w[pp_][1], f"b_w{pp_}1"),
                                     ("mr", f_mr, "f_mr"), ("rr", f_rr, "f_rr"), ("hcpre", s_hc, "s_hc")):
                    od = dt("dbgs_" + nm, list(tile.shape), tile.dtype, kind="ExternalOutput")
                    T("sync", lambda e, od=od, tile=tile: e.dma_start(out=od.ap(), in_=tile[:]),
                      r=(rk,), w=("dbg",), dma=True)
            SCAN_MAIN(0, s)
            if s == int(_os.environ.get("KDSW", "0")) and _os.environ.get("KDBG"):
                for nm, tile, rk in (("hc", s_hc, "s_hc"), ("y0", f_y[0], "f_y0"), ("y1", f_y[1], "f_y1"),
                                     ("acol", s_acol, "s_acol"), ("bB1", b_B[1], "b_B1"), ("bC1", b_C[1], "b_C1")):
                    od = dt("dbgs_" + nm, list(tile.shape), tile.dtype, kind="ExternalOutput")
                    T("sync", lambda e, od=od, tile=tile: e.dma_start(out=od.ap(), in_=tile[:]),
                      r=(rk,), w=("dbg",), dma=True)
        SCAN_COLL(0)
        P1(2)
        LN2_PRO(0)
        for g in range(1, G + 1):
            if g < G:
                for s in range(NSW):
                    SCAN_PREP(g, s)
                    SCAN_MAIN(g, s)
                    GATEUP(g - 1, s)
                SCAN_COLL(g)
                if g + 2 <= G - 1:
                    P1(g + 2)
                LN2_PRO(g)
                DOWN(g - 1)
            else:
                for s in range(NSW):
                    GATEUP(g - 1, s)
                DOWN(g - 1)

        if _os.environ.get("KDBG"):
            for (ot, srct, rk) in dbg_emit:
                T("sync", lambda e, ot=ot, srct=srct: e.dma_start(out=ot[tuple(slice(None) for _ in ot.shape)] if False else ot.ap(), in_=srct if isinstance(srct, bass.AP) else srct.ap()),
                  r=(rk,), w=("dbg",), dma=True)

        t.finalize_and_emit(nc, stack)
    return nc


def kernel(L0_attn_x, L0_attn_W_xproj, L0_attn_W_dt, L0_attn_b_dt, L0_attn_A_log,
           L0_attn_D_param, L0_ffn_gate_w, L0_ffn_up_w, L0_ffn_down_w,
           L0_ffn_gate_scale, L0_ffn_up_scale, L0_ffn_down_scale):
    f32 = np.float32
    x = np.asarray(L0_attn_x, f32)[0]
    wx = np.asarray(L0_attn_W_xproj, f32)
    wdt = np.asarray(L0_attn_W_dt, f32)
    b_dt = np.asarray(L0_attn_b_dt, f32)
    a_log = np.asarray(L0_attn_A_log, f32)
    d_par = np.asarray(L0_attn_D_param, f32)
    gw = np.asarray(L0_ffn_gate_w, f32)
    uw = np.asarray(L0_ffn_up_w, f32)
    dw = np.asarray(L0_ffn_down_w, f32)
    gs = np.asarray(L0_ffn_gate_scale, f32)
    us = np.asarray(L0_ffn_up_scale, f32)
    ds = np.asarray(L0_ffn_down_scale, f32)

    def blocked_gu(wt):
        return np.ascontiguousarray(
            wt.reshape(NDB, 128, NIC, 128).transpose(2, 1, 0, 3).reshape(NIC, 128, D))

    gwh = blocked_gu(gw)
    uwh = blocked_gu(uw)
    dwh = np.ascontiguousarray(dw.reshape(NIC, 128, D))
    gs_t = np.ascontiguousarray(gs.reshape(NIC, 128).T)
    us_t = np.ascontiguousarray(us.reshape(NIC, 128).T)
    ds_rep = np.ascontiguousarray(np.broadcast_to(ds[None, :], (128, D)))
    wxTh = np.ascontiguousarray(wx.T.reshape(NDB, 128, E).transpose(1, 0, 2))

    in_maps = []
    for c in range(W):
        dsl = slice(DSH * c, DSH * (c + 1))
        xfm_own = np.empty((G, NDB, 128, TB), f32)
        for g in range(G):
            t0 = TB * (W * g + c)
            xfm_own[g] = np.ascontiguousarray(x[t0:t0 + TB, :].T).reshape(NDB, 128, TB)
        xfmT = np.ascontiguousarray(x[:, dsl].T).reshape(NBLK, 128, L)
        in_maps.append(dict(
            xfm_own=xfm_own,
            xfmT=xfmT,
            wxTh=wxTh,
            wdtT=np.ascontiguousarray(wdt[dsl, :].T),
            bdt=np.ascontiguousarray(b_dt[dsl].reshape(NBLK, 128).T),
            alog=np.ascontiguousarray(
                a_log[dsl, :].reshape(NBLK, 128, NST).transpose(1, 0, 2).reshape(128, NBLK * NST)),
            dparam=np.ascontiguousarray(d_par[dsl].reshape(NBLK, 128).T),
            gwh=gwh, uwh=uwh, dwh=dwh,
            gs_t=gs_t, us_t=us_t, ds_rep=ds_rep,
        ))

    nc = build_nc()
    res = run_bass_kernel_spmd(nc, in_maps, core_ids=list(range(W)))

    import os as _os
    if _os.environ.get("KDBG"):
        kernel.dbg = [{k: np.asarray(v) for k, v in r.items() if k.startswith("dbg")} for r in res.results]
    out_full = np.empty((1, L, D), f32)
    for c in range(W):
        o = np.asarray(res.results[c]["out"])
        for g in range(G):
            t0 = TB * (W * g + c)
            out_full[0, t0:t0 + TB, :] = o[g].reshape(D, TB).T
    return out_full


# version 11
# speedup vs baseline: 1.1848x; 1.1848x over previous
import contextlib
import numpy as np
import concourse.bass as bass
import concourse.mybir as mybir
from concourse.bass_utils import run_bass_kernel_spmd

F32 = mybir.dt.float32
BF16 = mybir.dt.bfloat16
F8 = mybir.dt.float8e4
AF = mybir.ActivationFunctionType
ALU = mybir.AluOpType

W = 8
L = 16384
D = 2048
NST = 16
RNK = 64
E = 96
G = 4
WT = L // G
TB = 512
SW = 1024
NSW = WT // SW
DSH = 256
NBLK = 2
NIC = 48
NDB = 16
EPS = 1e-5

ENGINES = ("sync", "vector", "scalar", "gpsimd", "tensor")


class Track:

    def __init__(self, serialize=False):
        self.serialize = serialize
        self.ops = []

    def op(self, engine, fn, r=(), w=(), dma=False, cc=False):
        inc = 1 if cc else (16 if dma else 0)
        self.ops.append((engine, fn, tuple(r), tuple(w), inc))

    def finalize_and_emit(self, nc, stack):
        n = len(self.ops)
        mk_eng = []
        for (e, fn, r, wr, dinc) in self.ops:
            if dinc:
                assert len(wr) == 1, wr
                mk_eng.append("r_" + wr[0])
            else:
                mk_eng.append(e)
        last_write = {}
        readers = {}
        deps = [dict() for _ in range(n)]
        needed = [False] * n

        def add_dep(i, j):
            ei, ej = mk_eng[i], mk_eng[j]
            iss = self.ops[i][0]
            if ej == iss and self.ops[j][4] == 0:
                return
            cur = deps[i].get(ej, -1)
            if j > cur:
                deps[i][ej] = j

        for i, (e, fn, r, wr, dinc) in enumerate(self.ops):
            if self.serialize and i > 0:
                add_dep(i, i - 1)
            for b in r:
                j = last_write.get(b)
                if j is not None:
                    add_dep(i, j)
            for b in wr:
                j = last_write.get(b)
                if j is not None:
                    add_dep(i, j)
                for (re, j) in readers.get(b, {}).items():
                    add_dep(i, j)
            for b in wr:
                last_write[b] = i
                readers[b] = {}
            for b in r:
                readers.setdefault(b, {})[mk_eng[i]] = i
        for i in range(n):
            for (ej, j) in deps[i].items():
                needed[j] = True
        for i, (e, fn, r, wr, dinc) in enumerate(self.ops):
            if dinc:
                needed[i] = True
        mark_no = [0] * n
        cnt = {}
        for i in range(n):
            if needed[i]:
                inc = self.ops[i][4] or 1
                cnt[mk_eng[i]] = cnt.get(mk_eng[i], 0) + inc
                mark_no[i] = cnt[mk_eng[i]]
        sems = {k: stack.enter_context(nc.semaphore("s_" + k)) for k in cnt}
        block = stack.enter_context(nc.Block())

        known0 = {k: 0 for k in cnt}

        def make_section(ename):
            my_ops = [i for i in range(n) if self.ops[i][0] == ename]

            def section(eng):
                known = dict(known0)
                for i in my_ops:
                    for (ej, j) in sorted(self.ops_deps_items(i)):
                        v = mark_no[j]
                        assert v > 0, (i, j, ej)
                        if known.get(ej, 0) < v:
                            eng.wait_ge(sems[ej], v)
                            known[ej] = v
                    ins = self.ops[i][1](eng)
                    if needed[i]:
                        ins.then_inc(sems[mk_eng[i]], self.ops[i][4] or 1)
                if ename == "sync":
                    for (k, v) in sorted(cnt.items()):
                        eng.wait_ge(sems[k], v)
            return section

        self._deps = deps
        for ename in ENGINES:
            if any(o[0] == ename for o in self.ops) or ename == "sync":
                getattr(block, ename)(make_section(ename))

    def ops_deps_items(self, i):
        return self._deps[i].items()


def build_nc():
    nc = bass.Bass("TRN2", target_bir_lowering=False)
    dt = nc.dram_tensor

    xfm_own = dt("xfm_own", [G, NDB, 128, TB], F32, kind="ExternalInput").ap()
    xfmT = dt("xfmT", [NBLK, 128, L], F32, kind="ExternalInput").ap()
    wxTh = dt("wxTh", [128, NDB, E], F32, kind="ExternalInput").ap()
    wdtT = dt("wdtT", [RNK, DSH], F32, kind="ExternalInput").ap()
    bdt = dt("bdt", [128, NBLK], F32, kind="ExternalInput").ap()
    alog = dt("alog", [128, NBLK * NST], F32, kind="ExternalInput").ap()
    dparam = dt("dparam", [128, NBLK], F32, kind="ExternalInput").ap()
    gwh = dt("gwh", [NIC, 128, D], F32, kind="ExternalInput").ap()
    uwh = dt("uwh", [NIC, 128, D], F32, kind="ExternalInput").ap()
    dwh = dt("dwh", [NIC, 128, D], F32, kind="ExternalInput").ap()
    gs_t = dt("gs_t", [128, NIC], F32, kind="ExternalInput").ap()
    us_t = dt("us_t", [128, NIC], F32, kind="ExternalInput").ap()
    ds_rep = dt("ds_rep", [128, D], F32, kind="ExternalInput").ap()
    out = dt("out", [G, NDB, 128, TB], F32, kind="ExternalOutput").ap()

    wq_g = dt("wq_g", [NIC, 128, D], BF16).ap()
    wq_u = dt("wq_u", [NIC, 128, D], BF16).ap()
    wq_d = dt("wq_d", [NIC, 128, D], BF16).ap()
    xdbl_in = [dt(f"xdbl_in{g}", [E, TB], BF16) for g in range(G)]
    xdbl_out = [dt(f"xdbl_out{g}", [W * E, TB], BF16) for g in range(G)]
    st1_in = [dt(f"st1_in{g}", [2, TB], F32) for g in range(G)]
    st1_out = [dt(f"st1_out{g}", [2 * W, TB], F32) for g in range(G)]
    res1_in = [dt(f"res1_in{g}", [DSH, WT], BF16) for g in range(G)]
    res1_out = [dt(f"res1_out{g}", [W * DSH, WT], BF16) for g in range(G)]
    st2_in = [dt(f"st2_in{g}", [2, WT], F32) for g in range(G)]
    st2_out = [dt(f"st2_out{g}", [2, WT], F32) for g in range(G)]
    m2r2 = [dt(f"m2r2{g}", [2, WT], F32) for g in range(G)]
    rg = [list(range(W))]

    stack = contextlib.ExitStack()
    sb = lambda name, shape, dtype=F32: stack.enter_context(nc.sbuf_tensor(name, shape, dtype))
    ps = lambda name, shape: stack.enter_context(nc.psum_tensor(name, shape, F32))

    with stack:
        s_wx = sb("s_wx", [128, NDB * E], BF16)
        s_wdt = sb("s_wdt", [RNK, DSH], BF16)
        s_gs = sb("s_gs", [128, NIC])
        s_us = sb("s_us", [128, NIC])
        s_acol = sb("s_acol", [128, NBLK * NST])
        s_bdt = sb("s_bdt", [128, NBLK])
        s_dp = sb("s_dp", [128, NBLK])
        s_hc = sb("s_hc", [128, 2 * NBLK * NST])
        s_ones = sb("s_ones", [128, 1], BF16)
        _sd = sb("s_draw", [RNK, SW], BF16)
        s_draw = [_sd, _sd]

        f_x = [sb(f"f_x{i}", [128, SW]) for i in range(NBLK)]
        f_dl = [[sb(f"f_dl{p}{i}", [128, SW]) for i in range(NBLK)] for p in range(2)]
        f_t = [sb(f"f_t{i}", [128, SW]) for i in range(2)]
        f_mr = sb("f_mr", [128, SW])
        f_rr = sb("f_rr", [128, SW])
        f_dA = [sb(f"f_dA{i}", [128, SW]) for i in range(3)]
        f_y = [sb(f"f_y{i}", [128, SW]) for i in range(NBLK)]
        b_ln = [[sb(f"b_ln{p}{i}", [128, SW], BF16) for i in range(NBLK)] for p in range(2)]
        b_w = [[sb(f"b_w{p}{i}", [128, SW], BF16) for i in range(NBLK)] for p in range(2)]
        b_dx = [sb(f"b_dx{i}", [128, SW], BF16) for i in range(2)]
        b_h = [sb(f"b_h{i}", [128, SW], BF16) for i in range(2)]
        b_B = [sb(f"b_B{i}", [128, SW], BF16) for i in range(2)]
        b_C = [sb(f"b_C{i}", [128, SW], BF16) for i in range(2)]
        b_r1 = b_dx
        b_sq2 = b_h

        s_fuse = [sb(f"s_fuse{i}", [128, 8 * TB], BF16) for i in range(6)]
        fuse = [s_fuse[i // 8][:, (i % 8) * TB:(i % 8 + 1) * TB] for i in range(NIC)]
        s_ln2 = [sb(f"s_ln2{i}", [128, 8 * TB], BF16) for i in range(2)]
        ln2t = [s_ln2[i // 8][:, (i % 8) * TB:(i % 8 + 1) * TB] for i in range(NDB)]
        s_wg = [sb(f"s_wg{i}", [128, 1024], BF16) for i in range(4)]
        s_wu = [sb(f"s_wu{i}", [128, 1024], BF16) for i in range(4)]
        s_dwq = [sb(f"s_dwq{i}", [128, 1024], BF16) for i in range(2)]
        s_r1f = [sb(f"s_r1f{i}", [128, TB], BF16) for i in range(2)]
        _gs1 = sb("s_gsil0", [128, TB], BF16)
        s_gsil = [_gs1, _gs1]
        f_pt = sb("f_pt", [128, SW])
        qstage = s_ln2[0][:].bitcast(F32)[:, 0:D]
        dstage = s_ln2[1][:].bitcast(F32)[:, 0:D]
        s_q8 = s_fuse[2][:, 0:1024].bitcast(F8)

        s_xo = [sb(f"s_xo{i}", [128, TB]) for i in range(2)]
        s_t32 = [sb(f"s_t32{i}", [128, TB]) for i in range(2)]
        s_outt = s_t32
        s_x16 = s_gsil[0]
        s_sq16 = sb("s_sq16", [128, TB], BF16)
        s_xdbl = s_gsil[0][0:96, :]
        s_row = [sb(f"s_row{i}", [1, TB]) for i in range(2)]
        _mr1 = sb("s_mrep1", [128, 2 * TB])
        s_mrep1 = _mr1[:, 0:TB]
        s_rrep1 = _mr1[:, TB:2 * TB]
        s_m2rep = s_mrep1
        s_r2rep = s_rrep1
        s_s08 = s_xo[0][0:8, :]
        s_s18 = s_xo[1][0:8, :]
        s_m28 = s_t32[0][0:8, :]
        s_r28 = s_t32[1][0:8, :]

        p_A = ps("p_A", [128, 2048])
        p_B = ps("p_B", [128, 2048])

        import os as _os
        t = Track(serialize=bool(_os.environ.get("KSER")))
        T = t.op
        _cbox = {}

        def _coff():
            if "v" not in _cbox:
                _cbox["v"] = nc.partition_id() * TB
            return _cbox["v"]

        T("sync", lambda e: e.dma_start(out=qstage[:, 0:NDB * E], in_=wxTh.rearrange("p k e -> p (k e)")), r=(), w=("s_ln20",), dma=True)
        T("vector", lambda e: e.tensor_copy(s_wx[:], qstage[:, 0:NDB * E]), r=("s_ln20",), w=("s_wx",))
        T("sync", lambda e: e.dma_start(out=f_t[1][0:RNK, 0:DSH], in_=wdtT), r=(), w=("f_t1",), dma=True)
        T("vector", lambda e: e.tensor_copy(s_wdt[:], f_t[1][0:RNK, 0:DSH]), r=("f_t1",), w=("s_wdt",))
        T("sync", lambda e: e.dma_start(out=s_gs[:], in_=gs_t), w=("s_gs",), dma=True)
        T("sync", lambda e: e.dma_start(out=s_us[:], in_=us_t), w=("s_us",), dma=True)
        T("sync", lambda e: e.dma_start(out=dstage, in_=ds_rep), w=("s_ln21",), dma=True)
        T("sync", lambda e: e.dma_start(out=s_acol[:], in_=alog), w=("s_acol",), dma=True)
        T("sync", lambda e: e.dma_start(out=s_bdt[:], in_=bdt), w=("s_bdt",), dma=True)
        T("sync", lambda e: e.dma_start(out=s_dp[:], in_=dparam), w=("s_dp",), dma=True)
        T("scalar", lambda e: e.activation(s_acol[:], s_acol[:], AF.Exp), r=("s_acol",), w=("s_acol",))
        T("vector", lambda e: e.tensor_single_scalar(s_acol[:], s_acol[:], -1.0, ALU.mult), r=("s_acol",), w=("s_acol",))
        T("vector", lambda e: e.memset(s_ones[:], 1.0), w=("s_ones",))
        T("vector", lambda e: e.memset(s_hc[:], 0.0), w=("s_hc",))

        def P1(g):
            for db in range(NDB):
                xo, xok = s_xo[db % 2], f"s_xo{db % 2}"
                T("sync", lambda e, g=g, db=db, xo=xo: e.dma_start(out=xo[:], in_=xfm_own[g, db]),
                  r=(), w=(xok,), dma=True)
                T("vector", lambda e, xo=xo: e.tensor_copy(s_x16[:], xo[:]), r=(xok,), w=("s_gsil0",))
                T("scalar", lambda e, xo=xo: e.activation(s_sq16[:], xo[:], AF.Square), r=(xok,), w=("s_sq16",))
                T("tensor", lambda e, db=db: e.matmul(p_B[0:1, 1024:1024 + TB], s_ones[:], s_x16[:],
                                                      start=(db == 0), stop=(db == NDB - 1)),
                  r=("s_ones", "s_gsil0"), w=("pB2",))
                T("tensor", lambda e, db=db: e.matmul(p_B[0:1, 1536:1536 + TB], s_ones[:], s_sq16[:],
                                                      start=(db == 0), stop=(db == NDB - 1)),
                  r=("s_ones", "s_sq16"), w=("pB3",))
            T("vector", lambda e: e.tensor_single_scalar(s_row[0][:], p_B[0:1, 1024:1024 + TB], 1.0 / D, ALU.mult),
              r=("pB2",), w=("row0",))
            T("vector", lambda e: e.tensor_single_scalar(s_row[1][:], p_B[0:1, 1536:1536 + TB], 1.0 / D, ALU.mult),
              r=("pB3",), w=("row1",))
            T("scalar", lambda e, g=g: e.dma_start(out=st1_in[g][0:1, :], in_=s_row[0][:]),
              r=("row0",), w=(f"st1_in{g}",), dma=True)
            T("scalar", lambda e, g=g: e.activation(s_row[0][:], s_row[0][:], AF.Square), r=("row0", f"st1_in{g}"), w=("row0",))
            T("vector", lambda e: e.tensor_sub(s_row[1][:], s_row[1][:], s_row[0][:]), r=("row1", "row0"), w=("row1",))
            T("vector", lambda e: e.tensor_single_scalar(s_row[1][:], s_row[1][:], EPS, ALU.add), r=("row1",), w=("row1",))
            T("scalar", lambda e: e.activation(s_row[1][:], s_row[1][:], AF.Ln), r=("row1",), w=("row1",))
            T("vector", lambda e: e.tensor_single_scalar(s_row[1][:], s_row[1][:], -0.5, ALU.mult), r=("row1",), w=("row1",))
            T("scalar", lambda e: e.activation(s_row[1][:], s_row[1][:], AF.Exp), r=("row1",), w=("row1",))
            T("scalar", lambda e, g=g: e.dma_start(out=st1_in[g][1:2, :], in_=s_row[1][:]),
              r=("row1",), w=(f"st1_in{g}",), dma=True)
            if g == 0 and _os.environ.get("KDBG"):
                od = dt("dbgs_st1in", [2, TB], F32, kind="ExternalOutput")
                T("sync", lambda e, od=od, g=g: e.dma_start(out=od.ap(), in_=st1_in[g].ap()[:, :]),
                  r=(f"st1_in{g}",), w=("dbg",), dma=True)
                od2 = dt("dbgs_row01", [2, TB], F32, kind="ExternalOutput")
                T("scalar", lambda e, od2=od2: e.dma_start(out=od2.ap()[0:1, :], in_=s_row[0][:]),
                  r=("row0",), w=("dbg",), dma=True)
                T("scalar", lambda e, od2=od2: e.dma_start(out=od2.ap()[1:2, :], in_=s_row[1][:]),
                  r=("row1",), w=("dbg",), dma=True)
            T("sync", lambda e, g=g: e.dma_start(out=s_mrep1, in_=st1_in[g].ap()[0:1, :].broadcast_to([128, TB])),
              r=(f"st1_in{g}",), w=("s_mrep1",), dma=True)
            T("sync", lambda e, g=g: e.dma_start(out=s_rrep1, in_=st1_in[g].ap()[1:2, :].broadcast_to([128, TB])),
              r=(f"st1_in{g}",), w=("s_rrep1",), dma=True)
            for db in range(NDB):
                xo, xok = s_xo[db % 2], f"s_xo{db % 2}"
                t32, tk = s_t32[db % 2], f"s_t32{db % 2}"
                lk = ln1f_keys[db // 2]
                T("sync", lambda e, g=g, db=db, xo=xo: e.dma_start(out=xo[:], in_=xfm_own[g, db]),
                  r=(), w=(xok,), dma=True)
                T("vector", lambda e, xo=xo, t32=t32: e.tensor_sub(t32[:], xo[:], s_mrep1),
                  r=(xok, "s_mrep1"), w=(tk,))
                T("vector", lambda e, t32=t32, db=db: e.tensor_mul(ln1f[db], t32[:], s_rrep1),
                  r=(tk, "s_rrep1"), w=(lk,))
                T("tensor", lambda e, db=db: e.matmul(p_B[0:E, 512:512 + TB],
                                                      s_wx[:, db * E:(db + 1) * E], ln1f[db],
                                                      start=(db == 0), stop=(db == NDB - 1)),
                  r=("s_wx", lk), w=("pB1",))
            T("scalar", lambda e: e.activation(s_xdbl[:], p_B[0:E, 512:512 + TB], AF.Copy),
              r=("pB1",), w=("s_gsil0",))
            T("scalar", lambda e, g=g: e.dma_start(out=xdbl_in[g][:, :], in_=s_xdbl[:]),
              r=("s_gsil0",), w=(f"xdbl_in{g}",), dma=True)
            T("gpsimd", lambda e, g=g: e.collective_compute(
                "AllGather", ALU.bypass, replica_groups=rg,
                ins=[xdbl_in[g].ap().opt()], outs=[xdbl_out[g].ap().opt()]),
              r=(f"xdbl_in{g}",), w=(f"xdbl_out{g}",), cc=True)
            T("gpsimd", lambda e, g=g: e.collective_compute(
                "AllGather", ALU.bypass, replica_groups=rg,
                ins=[st1_in[g].ap().opt()], outs=[st1_out[g].ap().opt()]),
              r=(f"st1_in{g}",), w=(f"st1_out{g}",), cc=True)

        ln1f_pool = [b_B[0], b_B[1], b_C[0], b_C[1],
                     b_ln[0][0], b_ln[0][1], b_ln[1][0], b_ln[1][1]]
        ln1f_keys = ["b_B0", "b_B1", "b_C0", "b_C1",
                     "b_ln00", "b_ln01", "b_ln10", "b_ln11"]
        ln1f = []
        for i in range(NDB):
            ln1f.append(ln1f_pool[i // 2][:, (i % 2) * TB:(i % 2 + 1) * TB])

        def QUANT():
            for (wsrc, dst, dk, scaled) in ((gwh, wq_g, "wq_g", False), (uwh, wq_u, "wq_u", False),
                                            (dwh, wq_d, "wq_d", True)):
                for ic in range(NIC):
                    wqo = (s_fuse[0], s_fuse[1])[ic % 2][:, 0:D]
                    wk = f"s_fuse{ic % 2}"
                    T("scalar", lambda e, wsrc=wsrc, ic=ic: e.dma_start(out=qstage, in_=wsrc[ic]),
                      r=(), w=("s_ln20",), dma=True)
                    T("vector", lambda e: e.tensor_copy(s_q8, qstage), r=("s_ln20",), w=("s_fuse2",))
                    if scaled:
                        T("vector", lambda e, wqo=wqo: e.tensor_tensor(wqo, s_q8, dstage, ALU.mult),
                          r=("s_fuse2", "s_ln21"), w=(wk,))
                    else:
                        T("vector", lambda e, wqo=wqo: e.tensor_copy(wqo, s_q8), r=("s_fuse2",), w=(wk,))
                    T("scalar", lambda e, dst=dst, ic=ic, wqo=wqo: e.dma_start(out=dst[ic], in_=wqo),
                      r=(wk,), w=(dk,), dma=True)

        def SCAN_PREP(g, s):
            p = s % 2
            r0 = 4 * (s // 2)
            r0 = 2 * s
            dr, drk = s_draw[p], "s_draw"
            T("sync", lambda e, g=g, r0=r0, dr=dr: e.dma_start(
                out=dr[:].rearrange("e (r t) -> e r t", r=2),
                in_=xdbl_out[g].ap()[r0 * E:(r0 + 2) * E, :]
                .rearrange("(r e) t -> e r t", r=2)[0:RNK]),
              r=(f"xdbl_out{g}",), w=(drk,), dma=True)
            T("sync", lambda e, g=g, r0=r0: e.dma_start(
                out=f_mr[:].rearrange("p (r t) -> p r t", r=2),
                in_=st1_out[g].ap()[2 * r0:2 * (r0 + 2), :]
                .rearrange("(r two) t -> two r t", two=2)[0:1].broadcast_to([128, 2, TB])),
              r=(f"st1_out{g}",), w=("f_mr",), dma=True)
            T("sync", lambda e, g=g, r0=r0: e.dma_start(
                out=f_rr[:].rearrange("p (r t) -> p r t", r=2),
                in_=st1_out[g].ap()[2 * r0:2 * (r0 + 2), :]
                .rearrange("(r two) t -> two r t", two=2)[1:2].broadcast_to([128, 2, TB])),
              r=(f"st1_out{g}",), w=("f_rr",), dma=True)
            for blk in range(NBLK):
                fxb, fxk = f_x[blk], f"f_x{blk}"
                fdb, fdk = f_dl[p][blk], f"f_dl{p}{blk}"
                lnb, lnk = b_ln[p][blk], f"b_ln{p}{blk}"
                wb, wk = b_w[p][blk], f"b_w{p}{blk}"
                T("sync", lambda e, g=g, s=s, blk=blk, fxb=fxb: e.dma_start(
                    out=fxb[:], in_=xfmT[blk, :, g * WT + s * SW: g * WT + (s + 1) * SW]),
                  r=(), w=(fxk,), dma=True)
                for sl in range(2):
                    T("tensor", lambda e, blk=blk, sl=sl, dr=dr: e.matmul(
                        p_B[:, sl * 512:(sl + 1) * 512],
                        s_wdt[:, blk * 128:(blk + 1) * 128],
                        dr[:, sl * 512:(sl + 1) * 512],
                        start=True, stop=True),
                      r=("s_wdt", drk), w=(f"pB{sl}",))
                T("scalar", lambda e, blk=blk: e.activation(
                    f_t[0][:], p_B[:, 0:SW], AF.Abs, bias=s_bdt[:, blk:blk + 1]),
                  r=("pB0", "pB1", "s_bdt"), w=("f_t0",))
                T("scalar", lambda e: e.activation(f_t[1][:], f_t[0][:], AF.Exp, scale=-1.0),
                  r=("f_t0",), w=("f_t1",))
                T("scalar", lambda e: e.activation(f_t[0][:], f_t[1][:], AF.Ln, bias=1.0),
                  r=("f_t1",), w=("f_t0",))
                T("vector", lambda e, blk=blk: e.tensor_scalar(
                    f_t[1][:], p_B[:, 0:SW], s_bdt[:, blk:blk + 1], 0.0, ALU.add, ALU.max),
                  r=("pB0", "pB1", "s_bdt"), w=("f_t1",))
                T("vector", lambda e, fdb=fdb: e.tensor_add(fdb[:], f_t[0][:], f_t[1][:]),
                  r=("f_t0", "f_t1"), w=(fdk,))
                T("gpsimd", lambda e, fxb=fxb: e.tensor_sub(f_pt[:], fxb[:], f_mr[:]),
                  r=(fxk, "f_mr"), w=("f_pt",))
                T("gpsimd", lambda e, lnb=lnb: e.tensor_mul(lnb[:], f_pt[:], f_rr[:]),
                  r=("f_pt", "f_rr"), w=(lnk,))
                T("vector", lambda e, fdb=fdb, lnb=lnb, wb=wb: e.tensor_mul(wb[:], fdb[:], lnb[:]),
                  r=(fdk, lnk), w=(wk,))

        def SCAN_MAIN(g, s):
            p = s % 2
            r0 = 2 * s
            drk = "s_draw"
            iters = [(n, blk) for n in range(NST) for blk in range(NBLK)]
            LOOK = 2

            def emit_dA(k):
                n_, blk_ = iters[k]
                idx = blk_ * NST + n_
                q3 = k % 3
                fdb, fdk = f_dl[p][blk_], f"f_dl{p}{blk_}"
                dA, dAk = f_dA[q3], f"f_dA{q3}"
                T("scalar", lambda e, fdb=fdb, dA=dA, idx=idx: e.activation(
                    dA[:], fdb[:], AF.Exp, scale=s_acol[:, idx:idx + 1]),
                  r=(fdk, "s_acol"), w=(dAk,))

            for k0 in range(LOOK):
                emit_dA(k0)
            for k, (n, blk) in enumerate(iters):
                bB, bBk = b_B[n % 2], f"b_B{n % 2}"
                bC, bCk = b_C[n % 2], f"b_C{n % 2}"
                if blk == 0:
                    T("sync", lambda e, g=g, r0=r0, n=n, bB=bB: e.dma_start(
                        out=bB[:].rearrange("p (r t) -> p r t", r=2),
                        in_=xdbl_out[g].ap()[r0 * E:(r0 + 2) * E, :]
                        .rearrange("(r e) t -> e r t", r=2)[RNK + n:RNK + n + 1]
                        .broadcast_to([128, 2, TB])),
                      r=(f"xdbl_out{g}",), w=(bBk,), dma=True)
                    T("sync", lambda e, g=g, r0=r0, n=n, bC=bC: e.dma_start(
                        out=bC[:].rearrange("p (r t) -> p r t", r=2),
                        in_=xdbl_out[g].ap()[r0 * E:(r0 + 2) * E, :]
                        .rearrange("(r e) t -> e r t", r=2)[RNK + NST + n:RNK + NST + n + 1]
                        .broadcast_to([128, 2, TB])),
                      r=(f"xdbl_out{g}",), w=(bCk,), dma=True)
                idx = blk * NST + n
                q3 = k % 3
                q = blk % 2
                yb, yk = f_y[blk], f"f_y{blk}"
                dA, dAk = f_dA[q3], f"f_dA{q3}"
                dx, dxk = b_dx[q], f"b_dx{q}"
                h, hk = b_h[q], f"b_h{q}"
                wb, wk = b_w[p][blk], f"b_w{p}{blk}"
                T("vector", lambda e, wb=wb, bB=bB, dx=dx: e.tensor_mul(dx[:], wb[:], bB[:]),
                  r=(wk, bBk), w=(dxk,))
                T("vector", lambda e, dA=dA, dx=dx, h=h, idx=idx: e.tensor_tensor_scan(
                    h[:], dA[:], dx[:], s_hc[:, 2 * idx + 1:2 * idx + 2], ALU.mult, ALU.add),
                  r=(dAk, dxk, "s_hc"), w=(hk,))
                if k + LOOK < len(iters):
                    emit_dA(k + LOOK)
                T("scalar", lambda e, h=h, idx=idx: e.copy(
                    s_hc[:, 2 * idx:2 * idx + 2], h[:, SW - 2:SW]),
                  r=(hk,), w=("s_hc",))
                T("vector", lambda e, h=h, bC=bC, dx=dx: e.tensor_mul(dx[:], h[:], bC[:]),
                  r=(hk, bCk), w=(dxk,))
                if n == 0:
                    T("vector", lambda e, yb=yb, dx=dx: e.tensor_copy(yb[:], dx[:]),
                      r=(dxk,), w=(yk,))
                else:
                    T("gpsimd", lambda e, yb=yb, dx=dx: e.tensor_add(yb[:], yb[:], dx[:]),
                      r=(dxk, yk), w=(yk,))
                    if g == 0 and s == int(_os.environ.get("KDSW", "0")) and n <= 1 and blk == 0 and _os.environ.get("KDBG"):
                        for nm, tile, rk in ((f"dA_n{n}", dA, dAk), (f"h_n{n}", h, hk),
                                             (f"hC_n{n}", dx, dxk), (f"B_n{n}", bB, bBk)):
                            od = dt("dbgn_" + nm, list(tile.shape), tile.dtype, kind="ExternalOutput")
                            T("sync", lambda e, od=od, tile=tile: e.dma_start(out=od.ap(), in_=tile[:]),
                              r=(rk,), w=("dbg",), dma=True)
            for blk in range(NBLK):
                fxb, fxk = f_x[blk], f"f_x{blk}"
                lnb, lnk = b_ln[p][blk], f"b_ln{p}{blk}"
                yb, yk = f_y[blk], f"f_y{blk}"
                r1, r1k = b_r1[blk], f"b_dx{blk}"
                T("vector", lambda e, lnb=lnb, yb=yb, blk=blk: e.scalar_tensor_tensor(
                    yb[:], lnb[:], s_dp[:, blk:blk + 1], yb[:], ALU.mult, ALU.add),
                  r=(lnk, "s_dp", yk), w=(yk,))
                T("vector", lambda e, fxb=fxb, yb=yb, r1=r1: e.tensor_add(r1[:], fxb[:], yb[:]),
                  r=(fxk, yk), w=(r1k,))
                T("scalar", lambda e, r1=r1, blk=blk: e.activation(b_sq2[blk][:], r1[:], AF.Square),
                  r=(r1k,), w=(f"b_h{blk}",))
                T("scalar", lambda e, g=g, s=s, blk=blk, r1=r1: e.dma_start(
                    out=res1_in[g].ap()[blk * 128:(blk + 1) * 128, s * SW:(s + 1) * SW], in_=r1[:]),
                  r=(r1k,), w=(f"res1_in{g}",), dma=True)
            if g == 0 and s == 0 and _os.environ.get("KDBG"):
                for nm, tile, rk in (("sq0", b_sq2[0], "b_h0"), ("sq1", b_sq2[1], "b_h1"),
                                     ("r1d0", b_r1[0], "b_dx0"), ("r1d1", b_r1[1], "b_dx1")):
                    od = dt("dbgq_" + nm, list(tile.shape), tile.dtype, kind="ExternalOutput")
                    T("sync", lambda e, od=od, tile=tile: e.dma_start(out=od.ap(), in_=tile[:]),
                      r=(rk,), w=("dbg",), dma=True)
            for sl in range(2):
                for blk in range(NBLK):
                    r1, r1k = b_r1[blk], f"b_dx{blk}"
                    T("tensor", lambda e, r1=r1, sl=sl, blk=blk: e.matmul(
                        p_B[0:1, 1024:1536],
                        s_ones[:], r1[:, sl * 512:(sl + 1) * 512],
                        start=(blk == 0), stop=(blk == NBLK - 1)),
                      r=("s_ones", r1k), w=("pB2",))
                    sqt, sqk = b_sq2[blk], f"b_h{blk}"
                    T("tensor", lambda e, sqt=sqt, sl=sl, blk=blk: e.matmul(
                        p_B[0:1, 1536:2048],
                        s_ones[:], sqt[:, sl * 512:(sl + 1) * 512],
                        start=(blk == 0), stop=(blk == NBLK - 1)),
                      r=("s_ones", sqk), w=("pB3",))
                T("vector", lambda e, sl=sl: e.tensor_copy(s_row[0][:, 0:512], p_B[0:1, 1024:1536]),
                  r=("pB2",), w=("row0",))
                T("vector", lambda e, sl=sl: e.tensor_copy(s_row[1][:, 0:512], p_B[0:1, 1536:2048]),
                  r=("pB3",), w=("row1",))
                T("scalar", lambda e, g=g, s=s, sl=sl: e.dma_start(
                    out=st2_in[g].ap()[0:1, s * SW + sl * 512: s * SW + (sl + 1) * 512],
                    in_=s_row[0][:, 0:512]),
                  r=("row0",), w=(f"st2_in{g}",), dma=True)
                T("scalar", lambda e, g=g, s=s, sl=sl: e.dma_start(
                    out=st2_in[g].ap()[1:2, s * SW + sl * 512: s * SW + (sl + 1) * 512],
                    in_=s_row[1][:, 0:512]),
                  r=("row1",), w=(f"st2_in{g}",), dma=True)

        def SCAN_COLL(g):
            if g == 0 and _os.environ.get("KDBG"):
                od = dt("dbgs_st2in", [2, WT], F32, kind="ExternalOutput")
                T("sync", lambda e, od=od, g=g: e.dma_start(out=od.ap(), in_=st2_in[g].ap()[:, :]),
                  r=(f"st2_in{g}",), w=("dbg",), dma=True)
                od3 = dt("dbgs_res1in", [DSH, WT], BF16, kind="ExternalOutput")
                T("sync", lambda e, od3=od3, g=g: e.dma_start(out=od3.ap(), in_=res1_in[g].ap()[:, :]),
                  r=(f"res1_in{g}",), w=("dbg",), dma=True)
            T("gpsimd", lambda e, g=g: e.collective_compute(
                "AllReduce", ALU.add, replica_groups=rg,
                ins=[st2_in[g].ap().opt()], outs=[st2_out[g].ap().opt()]),
              r=(f"st2_in{g}",), w=(f"st2_out{g}",), cc=True)
            T("gpsimd", lambda e, g=g: e.collective_compute(
                "AllGather", ALU.bypass, replica_groups=rg,
                ins=[res1_in[g].ap().opt()], outs=[res1_out[g].ap().opt()]),
              r=(f"res1_in{g}",), w=(f"res1_out{g}",), cc=True)
            T("sync", lambda e, g=g: e.dma_start(
                out=s_s08[:], in_=st2_out[g].ap()[0:1, :].rearrange("one (c t) -> (one c) t", c=8)),
              r=(f"st2_out{g}",), w=("s_xo0",), dma=True)
            T("sync", lambda e, g=g: e.dma_start(
                out=s_s18[:], in_=st2_out[g].ap()[1:2, :].rearrange("one (c t) -> (one c) t", c=8)),
              r=(f"st2_out{g}",), w=("s_xo1",), dma=True)
            T("vector", lambda e: e.tensor_single_scalar(s_m28[:], s_s08[:], 1.0 / D, ALU.mult),
              r=("s_xo0",), w=("s_t320",))
            T("scalar", lambda e: e.activation(s_r28[:], s_m28[:], AF.Square), r=("s_t320",), w=("s_t321",))
            T("vector", lambda e: e.scalar_tensor_tensor(s_r28[:], s_s18[:], 1.0 / D, s_r28[:], ALU.mult, ALU.subtract),
              r=("s_xo1", "s_t321"), w=("s_t321",))
            T("vector", lambda e: e.tensor_single_scalar(s_r28[:], s_r28[:], EPS, ALU.add), r=("s_t321",), w=("s_t321",))
            T("scalar", lambda e: e.activation(s_r28[:], s_r28[:], AF.Ln), r=("s_t321",), w=("s_t321",))
            T("vector", lambda e: e.tensor_single_scalar(s_r28[:], s_r28[:], -0.5, ALU.mult), r=("s_t321",), w=("s_t321",))
            T("scalar", lambda e: e.activation(s_r28[:], s_r28[:], AF.Exp), r=("s_t321",), w=("s_t321",))
            T("scalar", lambda e, g=g: e.dma_start(
                out=m2r2[g].ap()[0:1, :].rearrange("one (c t) -> (one c) t", c=8), in_=s_m28[:]),
              r=("s_t320",), w=(f"m2r2{g}",), dma=True)
            T("scalar", lambda e, g=g: e.dma_start(
                out=m2r2[g].ap()[1:2, :].rearrange("one (c t) -> (one c) t", c=8), in_=s_r28[:]),
              r=("s_t321",), w=(f"m2r2{g}",), dma=True)

        def LN2_PRO(g):
            T("sync", lambda e, g=g: e.dma_start(
                out=s_m2rep[:], in_=m2r2[g].ap()[0:1, bass.ds(_coff(), TB)].broadcast_to([128, TB])),
              r=(f"m2r2{g}",), w=("s_mrep1",), dma=True)
            T("sync", lambda e, g=g: e.dma_start(
                out=s_r2rep[:], in_=m2r2[g].ap()[1:2, bass.ds(_coff(), TB)].broadcast_to([128, TB])),
              r=(f"m2r2{g}",), w=("s_rrep1",), dma=True)
            for db in range(NDB):
                rf, rfk = s_r1f[db % 2], f"s_r1f{db % 2}"
                lnk = f"s_ln2{db // 8}"
                T("sync", lambda e, g=g, db=db, rf=rf: e.dma_start(
                    out=rf[:], in_=res1_out[g].ap()[db * 128:(db + 1) * 128, bass.ds(_coff(), TB)]),
                  r=(f"res1_out{g}",), w=(rfk,), dma=True)
                T("vector", lambda e, rf=rf, db=db: e.tensor_sub(s_outt[0][:], rf[:], s_m2rep),
                  r=(rfk, "s_mrep1"), w=("s_t320",))
                T("vector", lambda e, db=db: e.tensor_mul(ln2t[db], s_outt[0][:], s_r2rep),
                  r=("s_t320", "s_rrep1"), w=(lnk,))

        def GATEUP(g, part):
            for ic in range(12 * part, 12 * part + 12):
                wgl, wgh_, wgk = s_wg[2 * (ic % 2)], s_wg[2 * (ic % 2) + 1], f"s_wg{ic % 2}"
                wul, wuh_, wuk = s_wu[2 * (ic % 2)], s_wu[2 * (ic % 2) + 1], f"s_wu{ic % 2}"
                gsil, gsk = s_gsil[0], "s_gsil0"
                pg = f"pA{2 * (ic % 2)}"
                pu = f"pA{2 * (ic % 2) + 1}"
                pgo = 1024 * (ic % 2)
                T("scalar", lambda e, ic=ic, wgl=wgl: e.dma_start(out=wgl[:], in_=wq_g[ic][:, 0:1024]),
                  r=("wq_g",), w=(wgk,), dma=True)
                T("scalar", lambda e, ic=ic, wgh_=wgh_: e.dma_start(out=wgh_[:], in_=wq_g[ic][:, 1024:2048]),
                  r=("wq_g",), w=(wgk,), dma=True)
                T("scalar", lambda e, ic=ic, wul=wul: e.dma_start(out=wul[:], in_=wq_u[ic][:, 0:1024]),
                  r=("wq_u",), w=(wuk,), dma=True)
                T("scalar", lambda e, ic=ic, wuh_=wuh_: e.dma_start(out=wuh_[:], in_=wq_u[ic][:, 1024:2048]),
                  r=("wq_u",), w=(wuk,), dma=True)
                for db in range(NDB):
                    wt = (wgl, wgh_)[db // 8]
                    T("tensor", lambda e, wt=wt, db=db, pgo=pgo: e.matmul(
                        p_A[:, pgo:pgo + TB], wt[:, (db % 8) * 128:(db % 8) * 128 + 128], ln2t[db],
                        start=(db == 0), stop=(db == NDB - 1)),
                      r=(wgk, f"s_ln2{db // 8}"), w=(pg,))
                for db in range(NDB):
                    wt = (wul, wuh_)[db // 8]
                    T("tensor", lambda e, wt=wt, db=db, pgo=pgo: e.matmul(
                        p_A[:, pgo + TB:pgo + 2 * TB], wt[:, (db % 8) * 128:(db % 8) * 128 + 128], ln2t[db],
                        start=(db == 0), stop=(db == NDB - 1)),
                      r=(wuk, f"s_ln2{db // 8}"), w=(pu,))
                T("scalar", lambda e, ic=ic, gsil=gsil, pgo=pgo: e.activation(
                    gsil[:], p_A[:, pgo:pgo + TB], AF.Silu, scale=s_gs[:, ic:ic + 1]),
                  r=(pg, "s_gs"), w=(gsk,))
                T("vector", lambda e, ic=ic, gsil=gsil, pgo=pgo: e.scalar_tensor_tensor(
                    fuse[ic], p_A[:, pgo + TB:pgo + 2 * TB], s_us[:, ic:ic + 1], gsil[:], ALU.mult, ALU.mult),
                  r=(pu, "s_us", gsk), w=(f"s_fuse{ic // 8}",))

        def DOWN(g):
            for half in range(2):
                for ic in range(NIC):
                    dwt, dwk = s_dwq[ic % 2], f"s_dwq{ic % 2}"
                    T("scalar", lambda e, ic=ic, half=half, dwt=dwt: e.dma_start(
                        out=dwt[:], in_=wq_d[ic][:, half * 1024:(half + 1) * 1024]),
                      r=("wq_d",), w=(dwk,), dma=True)
                    for j in range(8):
                        pt = (p_A, p_B)[j // 4]
                        pk = ("pA", "pB")[j // 4] + str(j % 4)
                        T("tensor", lambda e, dwt=dwt, j=j, ic=ic, pt=pt: e.matmul(
                            pt[:, (j % 4) * 512:(j % 4) * 512 + 512],
                            dwt[:, j * 128:(j + 1) * 128], fuse[ic],
                            start=(ic == 0), stop=(ic == NIC - 1)),
                          r=(dwk, f"s_fuse{ic // 8}"), w=(pk,))
                for j in range(8):
                    db = half * 8 + j
                    pt = (p_A, p_B)[j // 4]
                    pk = ("pA", "pB")[j // 4] + str(j % 4)
                    ot, otk = s_outt[j % 2], f"s_t32{j % 2}"
                    rf, rfk = s_r1f[db % 2], f"s_r1f{db % 2}"
                    T("sync", lambda e, g=g, db=db, rf=rf: e.dma_start(
                        out=rf[:], in_=res1_out[g].ap()[db * 128:(db + 1) * 128, bass.ds(_coff(), TB)]),
                      r=(f"res1_out{g}",), w=(rfk,), dma=True)
                    T("vector", lambda e, pt=pt, j=j, ot=ot, rf=rf: e.tensor_add(
                        ot[:], pt[:, (j % 4) * 512:(j % 4) * 512 + 512], rf[:]),
                      r=(pk, rfk), w=(otk,))
                    T("sync", lambda e, g=g, db=db, ot=ot: e.dma_start(out=out[g, db], in_=ot[:]),
                      r=(otk,), w=("out",), dma=True)

        if _os.environ.get("KDBG"):
            dbg_specs = [
                ("dbg_xdbl0", xdbl_out[0], [W * E, TB], BF16, "xdbl_out0"),
                ("dbg_st10", st1_out[0], [2 * W, TB], F32, "st1_out0"),
                ("dbg_res10", res1_out[0], [W * DSH, WT], BF16, "res1_out0"),
                ("dbg_st20", st2_out[0], [2, WT], F32, "st2_out0"),
                ("dbg_m2r20", m2r2[0], [2, WT], F32, "m2r20"),
                ("dbg_res11", res1_out[1], [W * DSH, WT], BF16, "res1_out1"),
                ("dbg_wq0", wq_g[0], [128, D], BF16, "wq_g"),
                ("dbg_wd0", wq_d[0], [128, D], BF16, "wq_d"),
            ]
            dbg_emit = []
            for (nm, srct, shp, dtp, rk) in dbg_specs:
                ot = dt(nm, shp, dtp, kind="ExternalOutput")
                dbg_emit.append((ot, srct, rk))

        P1(0)
        QUANT()
        P1(1)
        for s in range(NSW):
            SCAN_PREP(0, s)
            if s == int(_os.environ.get("KDSW", "0")) and _os.environ.get("KDBG"):
                dbg_sc = {}
                pp_ = s % 2
                for nm, tile, rk in (("dl0", f_dl[pp_][0], f"f_dl{pp_}0"), ("dl1", f_dl[pp_][1], f"f_dl{pp_}1"),
                                     ("ln0", b_ln[pp_][0], f"b_ln{pp_}0"), ("ln1", b_ln[pp_][1], f"b_ln{pp_}1"),
                                     ("w0", b_w[pp_][0], f"b_w{pp_}0"), ("w1", b_w[pp_][1], f"b_w{pp_}1"),
                                     ("mr", f_mr, "f_mr"), ("rr", f_rr, "f_rr"), ("hcpre", s_hc, "s_hc")):
                    od = dt("dbgs_" + nm, list(tile.shape), tile.dtype, kind="ExternalOutput")
                    T("sync", lambda e, od=od, tile=tile: e.dma_start(out=od.ap(), in_=tile[:]),
                      r=(rk,), w=("dbg",), dma=True)
            SCAN_MAIN(0, s)
            if s == int(_os.environ.get("KDSW", "0")) and _os.environ.get("KDBG"):
                for nm, tile, rk in (("hc", s_hc, "s_hc"), ("y0", f_y[0], "f_y0"), ("y1", f_y[1], "f_y1"),
                                     ("acol", s_acol, "s_acol"), ("bB1", b_B[1], "b_B1"), ("bC1", b_C[1], "b_C1")):
                    od = dt("dbgs_" + nm, list(tile.shape), tile.dtype, kind="ExternalOutput")
                    T("sync", lambda e, od=od, tile=tile: e.dma_start(out=od.ap(), in_=tile[:]),
                      r=(rk,), w=("dbg",), dma=True)
        SCAN_COLL(0)
        P1(2)
        LN2_PRO(0)
        for g in range(1, G + 1):
            if g < G:
                for s in range(NSW):
                    SCAN_PREP(g, s)
                    SCAN_MAIN(g, s)
                    GATEUP(g - 1, s)
                SCAN_COLL(g)
                if g + 2 <= G - 1:
                    P1(g + 2)
                LN2_PRO(g)
                DOWN(g - 1)
            else:
                for s in range(NSW):
                    GATEUP(g - 1, s)
                DOWN(g - 1)

        if _os.environ.get("KDBG"):
            for (ot, srct, rk) in dbg_emit:
                T("sync", lambda e, ot=ot, srct=srct: e.dma_start(out=ot[tuple(slice(None) for _ in ot.shape)] if False else ot.ap(), in_=srct if isinstance(srct, bass.AP) else srct.ap()),
                  r=(rk,), w=("dbg",), dma=True)

        t.finalize_and_emit(nc, stack)
    return nc


def kernel(L0_attn_x, L0_attn_W_xproj, L0_attn_W_dt, L0_attn_b_dt, L0_attn_A_log,
           L0_attn_D_param, L0_ffn_gate_w, L0_ffn_up_w, L0_ffn_down_w,
           L0_ffn_gate_scale, L0_ffn_up_scale, L0_ffn_down_scale):
    f32 = np.float32
    x = np.asarray(L0_attn_x, f32)[0]
    wx = np.asarray(L0_attn_W_xproj, f32)
    wdt = np.asarray(L0_attn_W_dt, f32)
    b_dt = np.asarray(L0_attn_b_dt, f32)
    a_log = np.asarray(L0_attn_A_log, f32)
    d_par = np.asarray(L0_attn_D_param, f32)
    gw = np.asarray(L0_ffn_gate_w, f32)
    uw = np.asarray(L0_ffn_up_w, f32)
    dw = np.asarray(L0_ffn_down_w, f32)
    gs = np.asarray(L0_ffn_gate_scale, f32)
    us = np.asarray(L0_ffn_up_scale, f32)
    ds = np.asarray(L0_ffn_down_scale, f32)

    def blocked_gu(wt):
        return np.ascontiguousarray(
            wt.reshape(NDB, 128, NIC, 128).transpose(2, 1, 0, 3).reshape(NIC, 128, D))

    gwh = blocked_gu(gw)
    uwh = blocked_gu(uw)
    dwh = np.ascontiguousarray(dw.reshape(NIC, 128, D))
    gs_t = np.ascontiguousarray(gs.reshape(NIC, 128).T)
    us_t = np.ascontiguousarray(us.reshape(NIC, 128).T)
    ds_rep = np.ascontiguousarray(np.broadcast_to(ds[None, :], (128, D)))
    wxTh = np.ascontiguousarray(wx.T.reshape(NDB, 128, E).transpose(1, 0, 2))

    in_maps = []
    for c in range(W):
        dsl = slice(DSH * c, DSH * (c + 1))
        xfm_own = np.empty((G, NDB, 128, TB), f32)
        for g in range(G):
            t0 = TB * (W * g + c)
            xfm_own[g] = np.ascontiguousarray(x[t0:t0 + TB, :].T).reshape(NDB, 128, TB)
        xfmT = np.ascontiguousarray(x[:, dsl].T).reshape(NBLK, 128, L)
        in_maps.append(dict(
            xfm_own=xfm_own,
            xfmT=xfmT,
            wxTh=wxTh,
            wdtT=np.ascontiguousarray(wdt[dsl, :].T),
            bdt=np.ascontiguousarray(b_dt[dsl].reshape(NBLK, 128).T),
            alog=np.ascontiguousarray(
                a_log[dsl, :].reshape(NBLK, 128, NST).transpose(1, 0, 2).reshape(128, NBLK * NST)),
            dparam=np.ascontiguousarray(d_par[dsl].reshape(NBLK, 128).T),
            gwh=gwh, uwh=uwh, dwh=dwh,
            gs_t=gs_t, us_t=us_t, ds_rep=ds_rep,
        ))

    nc = build_nc()
    res = run_bass_kernel_spmd(nc, in_maps, core_ids=list(range(W)))

    import os as _os
    if _os.environ.get("KDBG"):
        kernel.dbg = [{k: np.asarray(v) for k, v in r.items() if k.startswith("dbg")} for r in res.results]
    out_full = np.empty((1, L, D), f32)
    for c in range(W):
        o = np.asarray(res.results[c]["out"])
        for g in range(G):
            t0 = TB * (W * g + c)
            out_full[0, t0:t0 + TB, :] = o[g].reshape(D, TB).T
    return out_full
